# version 08
# baseline (speedup 1.0000x reference)
import numpy as np
import ml_dtypes
import concourse.bass as bass
import concourse.mybir as mybir
from concourse.bass_utils import run_bass_kernel_spmd

F32 = mybir.dt.float32
BF16 = mybir.dt.bfloat16
AF = mybir.ActivationFunctionType
OP = mybir.AluOpType
AX = mybir.AxisListType

NTOK = 4096
NT = 32
D = 1024
DC = 8
O = 4096
OC = 32
H = 16
DH = 64
EPS = 1e-5
LN2 = float(np.log(2.0))
INV6 = 1.0 / 6.0
STT = 256
NST = NTOK // STT
E2M1_SQ = [0.0625, 0.5625, 1.5625, 3.0625, 6.25, 12.25, 25.0]
ABLATE = set()


class Eng:
    _skip = {"wait_ge", "nop", "wait_op", "drain"}

    def __init__(self, e):
        object.__setattr__(self, "_e", e)
        object.__setattr__(self, "last", None)

    def __getattr__(self, n):
        a = getattr(self._e, n)
        if callable(a) and n not in Eng._skip and not n.startswith("_"):
            def w(*args, **kw):
                r = a(*args, **kw)
                if hasattr(r, "then_inc"):
                    object.__setattr__(self, "last", r)
                return r
            return w
        return a


class Sem:
    def __init__(self, stack, nc, name):
        self.h = stack.enter_context(nc.semaphore(name))
        self.n = 0

    def dma(self, instr):
        instr.then_inc(self.h, 16)
        self.n += 16
        return self.n

    def inc(self, instr):
        instr.then_inc(self.h, 1)
        self.n += 1
        return self.n


def build(taps=(), upto="FB"):
    taps = set(taps)
    _PH = ["W", "A", "S", "Q", "FA", "FB"]
    _lim = _PH.index(upto)
    def _ph(p):
        return _PH.index(p) <= _lim
    nc = bass.Bass("TRN2", target_bir_lowering=False,
                   detect_race_conditions=False,
                   dynamic_dma_scratch_size=16384)

    x_d = nc.dram_tensor("x", [NTOK, D], F32, kind="ExternalInput")
    wqkv_d = nc.dram_tensor("wqkv", [D, 3 * D], F32, kind="ExternalInput")
    gk_d = nc.dram_tensor("gk", [H, DH], F32, kind="ExternalInput")
    bk_d = nc.dram_tensor("bk", [H, DH], F32, kind="ExternalInput")
    gv_d = nc.dram_tensor("gv", [H, DH], F32, kind="ExternalInput")
    bv_d = nc.dram_tensor("bv", [H, DH], F32, kind="ExternalInput")
    wout_d = nc.dram_tensor("wout", [D, D], F32, kind="ExternalInput")
    bout_d = nc.dram_tensor("bout", [D], F32, kind="ExternalInput")
    gate_d = nc.dram_tensor("gate", [O, D], F32, kind="ExternalInput")
    up_d = nc.dram_tensor("up", [O, D], F32, kind="ExternalInput")
    down_d = nc.dram_tensor("down", [D, O], F32, kind="ExternalInput")
    sin_d = nc.dram_tensor("sin", [1], F32, kind="ExternalInput")
    sind_d = nc.dram_tensor("sind", [1], F32, kind="ExternalInput")
    out_d = nc.dram_tensor("out", [NTOK, D], F32, kind="ExternalOutput")

    def scratch(name, shape):
        kind = "ExternalOutput" if ("sc:" + name) in taps else "Internal"
        return nc.dram_tensor(name, shape, BF16, kind=kind)

    xq_sc = scratch("xq_sc", [NTOK, D])
    s_sc = scratch("s_sc", [NTOK, D])
    wg_sc = scratch("wg_sc", [O, D])
    wu_sc = scratch("wu_sc", [O, D])
    wd_sc = scratch("wd_sc", [OC, 128, D])
    hq_sc = scratch("hq_sc", [NTOK, O])
    xql_sc = nc.dram_tensor("xql_sc", [NTOK, D], BF16, kind="Internal")
    wgT_sc = nc.dram_tensor("wgT_sc", [DC, 128, O], BF16, kind="Internal")
    wgTl_sc = nc.dram_tensor("wgTl_sc", [DC, 128, O], BF16, kind="Internal")
    wuT_sc = nc.dram_tensor("wuT_sc", [DC, 128, O], BF16, kind="Internal")
    wuTl_sc = nc.dram_tensor("wuTl_sc", [DC, 128, O], BF16, kind="Internal")
    tapd = {}
    for t in taps:
        if t.startswith("f32:"):
            _, nm, r, c = t.split(":")
            tapd[nm] = nc.dram_tensor("tap_" + nm, [int(r), int(c)], F32,
                                      kind="ExternalOutput")
        elif t.startswith("b16:"):
            _, nm, r, c = t.split(":")
            tapd[nm] = nc.dram_tensor("tap_" + nm, [int(r), int(c)], BF16,
                                      kind="ExternalOutput")

    ident_d = nc.inline_tensor(np.eye(128, dtype=np.float32), name="ident_c")
    onesf_d = nc.inline_tensor(np.ones((1, 128), dtype=np.float32),
                               name="onesf_c")
    onesb_d = nc.inline_tensor(np.ones((1, 128), dtype=ml_dtypes.bfloat16),
                               name="onesb_c")

    from contextlib import ExitStack

    def sb(stk, name, shape, dt=F32):
        return stk.enter_context(nc.sbuf_tensor(name, shape, dt))

    def ps(stk, name):
        return stk.enter_context(nc.psum_tensor(name, [128, 512], F32))

    v = Eng(nc.vector)
    sc = Eng(nc.scalar)
    te = Eng(nc.tensor)
    gp = Eng(nc.gpsimd)
    sy = nc.sync

    S0 = ExitStack()
    with S0:
        LD = Sem(S0, nc, "LD")
        STO = Sem(S0, nc, "STO")
        TP = Sem(S0, nc, "TP")
        V = Sem(S0, nc, "V")
        SC = Sem(S0, nc, "SC")
        MM = Sem(S0, nc, "MM")
        GP = Sem(S0, nc, "GP")

        ident = sb(S0, "ident", [128, 128])
        onesf = sb(S0, "onesf", [1, 128])
        onesb = sb(S0, "onesb", [1, 128], BF16)
        gsx = {k: sb(S0, "gsx_" + k, [128, 1])[:, :]
               for k in ("gs6", "rgs", "hrgs")}
        gsh = {k: sb(S0, "gsh_" + k, [128, 1])[:, :]
               for k in ("gs6", "rgs", "hrgs")}
        gsw = [{k: sb(S0, f"gsw{w}_{k}", [128, 1])[:, :]
                for k in ("gs6", "rgs", "hrgs")} for w in range(3)]
        smt = [sb(S0, f"smt{i}", [128, 16]) for i in range(4)]
        rp = sb(S0, "rp", [128, 4])
        sinrow = sb(S0, "sinrow", [1, 2])

        LD.dma(gp.dma_start(ident[:, :], ident_d[:, :]))
        LD.dma(gp.dma_start(onesf[:, :], onesf_d[:, :]))
        LD.dma(gp.dma_start(onesb[:, :], onesb_d[:, :]))
        LD.dma(gp.dma_start(sinrow[0:1, 0:1], sin_d[None, :]))
        LD.dma(gp.dma_start(sinrow[0:1, 1:2], sind_d[None, :]))

        def spacer():
            v.tensor_copy(rp[:, 0:2], rp[:, 2:4])
            v.tensor_copy(rp[:, 2:4], rp[:, 0:2])

        def v_recip(dst, srcap):
            spacer()
            v.reciprocal(dst, srcap)
            spacer()

        def v_reduce(*a, **k):
            v.tensor_reduce(*a, **k)
            spacer()

        v.memset(rp[:, :], 1.0)

        def sp1():
            v.tensor_copy(rp[:, 0:2], rp[:, 2:4])

        def nr_rsqrt(var_e, out_r, ta, tb):
            v_recip(ta, var_e)
            V.inc(v.last)
            sc.wait_ge(V.h, V.n)
            SC.inc(sc.activation(out_r, ta, AF.Sqrt))
            v.wait_ge(SC.h, SC.n)
            v.tensor_tensor(ta, out_r, out_r, OP.mult)
            sp1()
            v.tensor_tensor(tb, ta, var_e, OP.mult)
            sp1()
            v.tensor_scalar(ta, tb, -0.5, 1.5, OP.mult, OP.add)
            sp1()
            v.tensor_tensor(tb, out_r, ta, OP.mult)
            sp1()
            v.tensor_tensor(ta, tb, tb, OP.mult)
            sp1()
            v.tensor_tensor(out_r, ta, var_e, OP.mult)
            sp1()
            v.tensor_scalar(ta, out_r, -0.5, 1.5, OP.mult, OP.add)
            sp1()
            v.tensor_tensor(out_r, tb, ta, OP.mult)
            sp1()

        def scale_chain(amax, ch, g, narrow=False):
            sp = sp1
            scv, eacc, ulp, rulp, y, q8 = (ch[k] for k in
                                           ("scv", "eacc", "ulp", "rulp",
                                            "y", "q8"))
            v.tensor_scalar(scv, amax, g["gs6"], 448.0, OP.mult, OP.min)
            sp()
            v.tensor_scalar(eacc, scv, 2.0 ** -5, None, OP.is_ge)
            sp()
            pp = [eacc, y]
            i = 0
            for k in range(-4, 9):
                v.scalar_tensor_tensor(pp[1 - i], scv, float(2.0 ** k),
                                       pp[i], OP.is_ge, OP.add)
                sp()
                i = 1 - i
            e = pp[i]
            S1 = pp[1 - i]
            v.tensor_scalar(S1, e, 8.0, None, OP.is_ge)
            sp()
            v.tensor_scalar(ulp, S1, 255.0, 1.0, OP.mult, OP.add)
            v.scalar_tensor_tensor(rulp, S1, -8.0, e, OP.mult, OP.add)
            sp()
            v.tensor_scalar(S1, rulp, 4.0, None, OP.is_ge)
            sp()
            v.tensor_scalar(e, S1, 15.0, 1.0, OP.mult, OP.add)
            sp()
            v.tensor_tensor(ch["dq_s"], ulp, e, OP.mult)
            v.scalar_tensor_tensor(ulp, S1, -4.0, rulp, OP.mult, OP.add)
            sp()
            v.tensor_scalar(S1, ulp, 2.0, None, OP.is_ge)
            sp()
            v.tensor_scalar(e, S1, 3.0, 1.0, OP.mult, OP.add)
            v.scalar_tensor_tensor(rulp, S1, -2.0, ulp, OP.mult, OP.add)
            sp()
            v.tensor_scalar(S1, rulp, 1.0, None, OP.is_ge)
            sp()
            v.tensor_scalar(ulp, S1, 1.0, 1.0, OP.mult, OP.add)
            sp()
            v.tensor_tensor(rulp, e, ulp, OP.mult)
            sp()
            v.scalar_tensor_tensor(S1, ch["dq_s"], 2.0 ** -9, rulp,
                                   OP.mult, OP.mult)
            ulp_v = S1
            rulp_v = ulp
            v_recip(rulp_v, ulp_v)
            yv = pp[i]
            v.tensor_tensor(yv, scv, rulp_v, OP.mult)
            sp()
            v.tensor_scalar(q8, yv, 0.5, None, OP.is_ge)
            sp()
            pp2 = [q8, scv]
            i = 0
            for k in range(1, 16):
                v.scalar_tensor_tensor(pp2[1 - i], yv, k + 0.5, pp2[i],
                                       OP.is_ge, OP.add)
                sp()
                i = 1 - i
            sc8 = pp2[1 - i]
            v.tensor_tensor(sc8, pp2[i], ulp_v, OP.mult)
            sp()
            sp()
            v.tensor_scalar(ch["dq_s"], sc8, g["rgs"], 2.0 ** -35,
                            OP.mult, OP.max)
            v_recip(ch["rdq"], ch["dq_s"])
            v.tensor_tensor(ch["rdq2"], ch["rdq"], ch["rdq"], OP.mult)
            sp()
            v.tensor_scalar(ch["hdq"], sc8, g["hrgs"], None, OP.mult)

        def value_ladder(xsq, sign, rdq2, hdq, out, A, Bb, C, Dd, W,
                         out_lo=None):
            K = W // 16
            r3 = lambda ap: ap.rearrange("p (k b) -> p k b", b=16)
            rdq2_b = rdq2.unsqueeze(2).broadcast_to((128, K, 16))
            hdq_b = hdq.unsqueeze(2).broadcast_to((128, K, 16))
            v.tensor_tensor(r3(A), r3(xsq), rdq2_b, OP.mult)
            v.tensor_scalar(Bb, A, E2M1_SQ[0], None, OP.is_ge)
            v.scalar_tensor_tensor(C, A, E2M1_SQ[1], Bb, OP.is_ge, OP.add)
            v.scalar_tensor_tensor(Bb, A, E2M1_SQ[2], C, OP.is_ge, OP.add)
            v.scalar_tensor_tensor(C, A, E2M1_SQ[3], Bb, OP.is_ge, OP.add)
            v.tensor_scalar(Bb, A, E2M1_SQ[6], 2.0, OP.is_ge, OP.mult)
            v.scalar_tensor_tensor(Dd, A, E2M1_SQ[5], Bb, OP.is_ge, OP.add)
            v.scalar_tensor_tensor(Bb, A, E2M1_SQ[4], Dd, OP.is_ge, OP.add)
            v.scalar_tensor_tensor(Dd, Bb, 2.0, C, OP.mult, OP.add)
            v.tensor_copy(Bb, sign)
            v.tensor_tensor(r3(A), r3(Bb), hdq_b, OP.mult)
            if out_lo is None:
                v.tensor_tensor(out, Dd, A, OP.mult)
            else:
                v.tensor_tensor(C, Dd, A, OP.mult)
                v.tensor_copy(out, C)
                v.tensor_copy(Bb, out)
                v.tensor_tensor(out_lo, C, Bb, OP.subtract)

        with ExitStack() as SW:
          if _ph("W"):
            bc_ps = ps(SW, "bc_ps")
            tp_ps = ps(SW, "tp_ps")
            wld = [sb(SW, f"wld{i}", [128, 4096]) for i in range(2)]
            wsq = sb(SW, "wsq", [128, 4096])
            wsign = sb(SW, "wsign", [128, 4096], BF16)
            wq = [sb(SW, f"wq{i}", [128, 4096], BF16) for i in range(2)]
            lA = sb(SW, "lA", [128, 1024])
            lB = sb(SW, "lB", [128, 1024])
            lC = sb(SW, "lC", [128, 1024])
            lD = sb(SW, "lD", [128, 1024])
            colmax = sb(SW, "colmax", [128, 32])
            gcol = sb(SW, "gcol", [128, 1])
            gmrow = sb(SW, "gmrow", [1, 1])
            gmrow2 = sb(SW, "gmrow2", [1, 128])
            ch = {k: sb(SW, "ch_" + k, [128, 256])
                  for k in ("scv", "eacc", "ulp", "rulp", "y", "q8",
                            "dq_s", "rdq", "rdq2")}
            ch["hdq"] = sb(SW, "ch_hdq", [128, 256])
            amaxb = sb(SW, "amaxb", [128, 256])
            wtp = sb(SW, "wtp", [128, 4096], BF16)
            wtpl = sb(SW, "wtpl", [128, 1024], BF16)
            wql = [sb(SW, f"wql{i}", [128, 1024], BF16) for i in range(2)]

            te.wait_ge(LD.h, LD.n)
            MM.inc(te.matmul(bc_ps[:, 0:2], onesf[0:1, :], sinrow[0:1, 0:2],
                             start=True, stop=True))
            sc.wait_ge(MM.h, MM.n)
            SC.inc(sc.activation(smt[3][:, 0:2], bc_ps[:, 0:2], AF.Copy))
            v.wait_ge(SC.h, SC.n)
            for j, g in ((0, gsx), (1, gsh)):
                v.tensor_scalar(g["gs6"], smt[3][:, j:j + 1], INV6, None,
                                OP.mult)
                v_recip(g["rgs"], smt[3][:, j:j + 1])
                v.tensor_scalar(g["hrgs"], g["rgs"], 0.5, None, OP.mult)
            V.inc(v.last)

            WDEFS = [(gate_d, O, D, wg_sc, "split", wgT_sc, wgTl_sc),
                     (up_d, O, D, wu_sc, "split", wuT_sc, wuTl_sc),
                     (down_d, D, O, wd_sc, "packed", None, None)]
            for wi, (w_d, RR, CW, w_sc, mode, wT_sc, wTl_sc) in \
                    enumerate(WDEFS):
                packed = mode == "packed"
                ntile = RR // 128
                vdA = [0] * ntile
                for t in range(ntile):
                    buf = wld[t % 2]
                    if t >= 2:
                        gp.wait_ge(V.h, vdA[t - 2])
                    LD.dma(gp.dma_start(buf[:, 0:CW],
                                        w_d[t * 128:(t + 1) * 128, :]))
                    v.wait_ge(LD.h, LD.n)
                    v_reduce(colmax[:, t:t + 1], buf[:, 0:CW], AX.X,
                                    OP.max, apply_absolute_value=True)
                    vdA[t] = V.inc(v.last)
                v_reduce(gcol[:, :], colmax[:, 0:ntile], AX.X, OP.max)
                V.inc(v.last)
                te.wait_ge(V.h, V.n)
                MM.inc(te.transpose(tp_ps[0:1, 0:128], gcol[:, :],
                                    ident[:, :]))
                sc.wait_ge(MM.h, MM.n)
                SC.inc(sc.activation(gmrow2[0:1, :], tp_ps[0:1, 0:128],
                                     AF.Copy))
                v.wait_ge(SC.h, SC.n)
                v_reduce(gmrow[0:1, 0:1], gmrow2[0:1, :], AX.X,
                                OP.max)
                V.inc(v.last)
                te.wait_ge(V.h, V.n)
                MM.inc(te.matmul(bc_ps[:, 4:5], onesf[0:1, :],
                                 gmrow[0:1, 0:1], start=True, stop=True))
                sc.wait_ge(MM.h, MM.n)
                SC.inc(sc.activation(smt[3][:, 0:1], bc_ps[:, 4:5], AF.Copy))
                v.wait_ge(SC.h, SC.n)
                g = gsw[wi]
                v.tensor_scalar(smt[0][:, 0:1], smt[3][:, 0:1], 1e-30, None,
                                OP.max)
                v_recip(smt[1][:, 0:1], smt[0][:, 0:1])
                v.tensor_scalar(smt[2][:, 0:1], smt[1][:, 0:1], 2688.0, None,
                                OP.mult)
                sp1()
                v.tensor_scalar(g["gs6"], smt[2][:, 0:1], INV6, None, OP.mult)
                v_recip(g["rgs"], smt[2][:, 0:1])
                v.tensor_scalar(g["hrgs"], g["rgs"], 0.5, None, OP.mult)
                V.inc(v.last)
                if wi == 0 and "f32:gdbg:128:16" in taps:
                    gp.wait_ge(V.h, V.n)
                    for col, ap_ in ((0, g["gs6"]), (1, g["rgs"]),
                                     (2, g["hrgs"]), (3, smt[3][:, 0:1]),
                                     (4, smt[0][:, 0:1]), (5, smt[1][:, 0:1]),
                                     (6, smt[2][:, 0:1]), (7, gcol[:, 0:1])):
                        STO.dma(gp.dma_start(tapd["gdbg"][:, col:col + 1], ap_,
                                             allow_slow_non_contiguous=True))
                    STO.dma(gp.dma_start(tapd["gdbg"][0:1, 8:9],
                                         gmrow[0:1, 0:1]))
                    STO.dma(gp.dma_start(tapd["gdbg"][0:1, 9:14],
                                         gmrow2[0:1, 0:5]))

                KW = CW // 16
                vdB = [V.n] * ntile
                scB = [SC.n] * ntile
                stB = [STO.n] * ntile
                tpB = [TP.n] * ntile
                for t in range(ntile):
                    buf = wld[t % 2]
                    wqb = wq[t % 2]
                    if t >= 2:
                        gp.wait_ge(V.h, vdB[t - 2])
                        gp.wait_ge(SC.h, scB[t - 2])
                    LD.dma(gp.dma_start(buf[:, 0:CW],
                                        w_d[t * 128:(t + 1) * 128, :]))
                    sc.wait_ge(LD.h, LD.n)
                    sc.activation(wsq[:, 0:CW], buf[:, 0:CW], AF.Square)
                    scB[t] = SC.inc(sc.activation(wsign[:, 0:CW],
                                                  buf[:, 0:CW], AF.Sign))
                    v.wait_ge(LD.h, LD.n)
                    v_reduce(
                        amaxb[:, 0:KW],
                        buf[:, 0:CW].rearrange("p (k b) -> p k b", b=16),
                        AX.X, OP.max, apply_absolute_value=True)
                    scale_chain(amaxb[:, 0:KW],
                                {k: ch[k][:, 0:KW] for k in ch}, g)
                    if wi == 0 and t == 0 and "f32:wdbg:128:1024" in taps:
                        V.inc(v.last)
                        gp.wait_ge(V.h, V.n)
                        for nm, buf_ in (("a", amaxb), ("r", ch["rdq2"]),
                                         ("q", ch["q8"]), ("u", ch["ulp"])):
                            off = {"a": 0, "r": 256, "q": 512, "u": 768}[nm]
                            STO.dma(gp.dma_start(
                                tapd["wdbg"][:, off:off + 256], buf_[:, 0:256]))
                    v.wait_ge(SC.h, scB[t])
                    if t >= 2:
                        v.wait_ge(STO.h, stB[t - 2])
                        if mode != "plain":
                            v.wait_ge(TP.h, tpB[t - 2])
                    for sl in range(CW // 1024):
                        a, b = sl * 1024, (sl + 1) * 1024
                        value_ladder(wsq[:, a:b], wsign[:, a:b],
                                     ch["rdq2"][:, sl * 64:(sl + 1) * 64],
                                     ch["hdq"][:, sl * 64:(sl + 1) * 64],
                                     wqb[:, a:b], lA[:, :], lB[:, :],
                                     lC[:, :], lD[:, :], 1024,
                                     out_lo=(wql[t % 2][:, :]
                                             if mode == "split" else None))
                    vdB[t] = V.inc(v.last)
                    if mode == "split":
                        sy.wait_ge(V.h, vdB[t])
                        if t >= 1:
                            sy.wait_ge(STO.h, stB[t - 1])
                        TP.dma(sy.dma_start(
                            wtp[:, 0:1024].rearrange("p (m r) -> p m r", m=8),
                            wqb[:, 0:1024], transpose=True))
                        tpB[t] = TP.dma(sy.dma_start(
                            wtpl[:, :].rearrange("p (m r) -> p m r", m=8),
                            wql[t % 2][:, :], transpose=True))
                        gp.wait_ge(TP.h, tpB[t])
                        STO.dma(gp.dma_start(
                            wT_sc[:, :, t * 128:(t + 1) * 128]
                            .rearrange("dc p h -> p dc h"),
                            wtp[:, 0:1024].rearrange("p (m r) -> p m r", m=8)))
                        stB[t] = STO.dma(gp.dma_start(
                            wTl_sc[:, :, t * 128:(t + 1) * 128]
                            .rearrange("dc p h -> p dc h"),
                            wtpl[:, :].rearrange("p (m r) -> p m r", m=8)))
                        if ("sc:wg_sc" if wi == 0 else "sc:wu_sc") in taps:
                            stB[t] = STO.dma(gp.dma_start(
                                w_sc[t * 128:(t + 1) * 128, :], wqb[:, 0:CW]))
                    else:
                        sy.wait_ge(V.h, vdB[t])
                        if t >= 1:
                            sy.wait_ge(STO.h, stB[t - 1])
                        tpB[t] = TP.dma(sy.dma_start(
                            wtp[:, :].rearrange("p (m r) -> p m r", m=OC),
                            wqb[:, 0:CW], transpose=True))
                        gp.wait_ge(TP.h, tpB[t])
                        stB[t] = STO.dma(gp.dma_start(
                            wd_sc[:, :, t * 128:(t + 1) * 128]
                            .rearrange("oc p h -> p oc h"),
                            wtp[:, :].rearrange("p (m r) -> p m r", m=OC)))
            nc.all_engine_barrier()

        with ExitStack() as SA:
          if _ph("A"):
            xn_fm = sb(SA, "xn_fm", [128, DC * NTOK], BF16)
            wqkv_sb = sb(SA, "wqkv_sb", [128, DC * 3072], BF16)
            wout_sb = sb(SA, "wout_sb", [128, DC * 1024], BF16)
            W2_sb = sb(SA, "W2_sb", [128, DC * 1024], BF16)
            boutb = sb(SA, "boutb", [1, 1024], BF16)
            sacc = [sb(SA, f"sacc{i}", [128, H * 65]) for i in range(2)]
            st_ps = [ps(SA, f"stp{i}") for i in range(3)]

            with ExitStack() as SAL:
                kv_ps = [ps(SAL, f"kvp{i}") for i in range(4)]
                xld = [sb(SAL, f"xld{i}", [128, D]) for i in range(2)]
                xnt = sb(SAL, "xnt", [128, D])
                xnb = sb(SAL, "xnb", [128, D], BF16)
                xsq = sb(SAL, "xsq", [128, D])
                xsgn = sb(SAL, "xsgn", [128, D], BF16)
                xqt = sb(SAL, "xqt", [128, D], BF16)
                xqlt = sb(SAL, "xqlt", [128, D], BF16)
                lA = sb(SAL, "lA2", [128, 1024])
                lB = sb(SAL, "lB2", [128, 1024])
                lC = sb(SAL, "lC2", [128, 1024])
                lD = sb(SAL, "lD2", [128, 1024])
                ch = {k: lC[:, i * 64:(i + 1) * 64]
                      for i, k in enumerate(("scv", "eacc", "ulp", "rulp",
                                             "y", "q8", "dq_s", "rdq"))}
                ch["rdq2"] = sb(SAL, "ca_rdq2", [128, 64])[:, :]
                ch["hdq"] = sb(SAL, "ca_hdq", [128, 64])[:, :]
                amaxa = sb(SAL, "amaxa", [128, 64])
                bnst = sb(SAL, "bnst", [128, 12])
                bnag = sb(SAL, "bnag", [128, 2])
                sums = sb(SAL, "sums", [128, 4 * H])
                muk = sb(SAL, "muk", [128, H])
                rk = sb(SAL, "rk", [128, H])
                muv = sb(SAL, "muv", [128, H])
                rvv = sb(SAL, "rvv", [128, H])
                KT = [sb(SAL, f"KT{i}", [128, H * 66], BF16) for i in range(2)]
                VT = [sb(SAL, f"VT{i}", [128, H * 66], BF16) for i in range(2)]

                sc_w = SC.n
                with ExitStack() as SLD:
                  if "wq" not in ABLATE:
                    wstg = sb(SLD, "wstg", [128, 512])
                    for dc in range(DC):
                        for piece in range(8):
                            if dc + piece >= 1:
                                gp.wait_ge(SC.h, sc_w)
                            if piece < 6:
                                LD.dma(gp.dma_start(
                                    wstg[:, :],
                                    wqkv_d[dc * 128:(dc + 1) * 128,
                                           piece * 512:(piece + 1) * 512]))
                                dst = wqkv_sb[:, dc * 3072 + piece * 512:
                                              dc * 3072 + (piece + 1) * 512]
                            else:
                                pp = piece - 6
                                LD.dma(gp.dma_start(
                                    wstg[:, :],
                                    wout_d[dc * 128:(dc + 1) * 128,
                                           pp * 512:(pp + 1) * 512]))
                                dst = wout_sb[:, dc * 1024 + pp * 512:
                                              dc * 1024 + (pp + 1) * 512]
                            sc.wait_ge(LD.h, LD.n)
                            sc_w = SC.inc(sc.copy(dst, wstg[:, :]))
                    nc.all_engine_barrier()
                if "ms" not in ABLATE:
                    for b in range(2):
                        for tb in (KT, VT):
                            gp.memset(tb[b][:, :].rearrange(
                                "p (h w) -> p h w", w=66)[:, :, 64:66], 1.0)
                GP.inc(gp.last)
                gp_ones = GP.n

                r3h64 = lambda ap: ap.rearrange("p (h w) -> p h w", w=64)
                vdt = [0] * (NT + 2)
                sc_ev_t = [0] * (NT + 2)
                mmsc = [0] * (NT + 2)
                ld_tile = []
                DQ = TP if "syncdma" in ABLATE else LD
                dq_eng = sy if "syncdma" in ABLATE else gp
                if "x0" not in ABLATE:
                    DQ.dma(dq_eng.dma_start(xld[0][:, :], x_d[0:128, :]))
                ld_tile.append(DQ.n)
                stq = [0] * (NT + 2)
                te.wait_ge(SC.h, sc_w)
                te.wait_ge(GP.h, gp_ones)
                NT_RUN = NT
                for f in ABLATE:
                    if f.startswith("nt"):
                        NT_RUN = int(f[2:])
                for t in range(NT_RUN):
                    buf = xld[t % 2]
                    if t + 1 < NT:
                        if t >= 1:
                            dq_eng.wait_ge(V.h, vdt[t - 1])
                        DQ.dma(dq_eng.dma_start(
                            xld[(t + 1) % 2][:, :],
                            x_d[(t + 1) * 128:(t + 2) * 128, :]))
                        ld_tile.append(DQ.n)
                    if "ln" in ABLATE:
                        v.wait_ge(DQ.h, ld_tile[t])
                        v.tensor_copy(xnt[:, :], buf[:, :])
                        if t >= 1:
                            v.wait_ge(TP.h, tp_t)
                        v.tensor_copy(xnb[:, :], xnt[:, :])
                        v_xn = V.inc(v.last)
                    else:
                      v.wait_ge(DQ.h, ld_tile[t])
                      v.bn_stats(bnst[:, 0:6], buf[:, 0:512])
                      v.bn_stats(bnst[:, 6:12], buf[:, 512:1024])
                      spacer()
                      v.bn_aggr(bnag[:, :], bnst[:, :])
                      spacer()
                      v.tensor_scalar(smt[0][:, 0:1], bnag[:, 1:2], EPS, None,
                                      OP.add)
                      nr_rsqrt(smt[0][:, 0:1], smt[2][:, 0:1], smt[1][:, 0:1],
                               smt[3][:, 0:1])
                      v.tensor_scalar(xnt[:, :], buf[:, :], bnag[:, 0:1],
                                      smt[2][:, 0:1], OP.subtract, OP.mult)
                      if t >= 1:
                          v.wait_ge(TP.h, tp_t)
                      v.tensor_copy(xnb[:, :], xnt[:, :])
                      v_xn = V.inc(v.last)
                    if "tp" in ABLATE:
                        tp_t = TP.n
                    else:
                      sy.wait_ge(V.h, v_xn)
                      tp_t = TP.dma(sy.dma_start(
                          xn_fm[:, :].rearrange("p (m n) -> p m n", m=DC)
                          [:, :, t * 128:(t + 1) * 128],
                          xnb[:, :], transpose=True))
                    if "xq" in ABLATE:
                        stq[t] = STO.n
                        v_xq = V.n
                    else:
                      sc.wait_ge(V.h, v_xn)
                      sc.activation(xsq[:, :], xnt[:, :], AF.Square)
                      SC.inc(sc.activation(xsgn[:, :], xnt[:, :], AF.Sign))
                      sc_q = SC.n
                      v_reduce(
                        amaxa[:, :],
                        xnt[:, :].rearrange("p (k b) -> p k b", b=16),
                        AX.X, OP.max, apply_absolute_value=True)
                      scale_chain(amaxa[:, :], ch, gsx)
                      v.wait_ge(SC.h, sc_q)
                      if t >= 1:
                          v.wait_ge(TP.h if "syncdma" in ABLATE else STO.h,
                                    stq[t - 1])
                      value_ladder(xsq[:, :], xsgn[:, :], ch["rdq2"],
                                   ch["hdq"], xqt[:, :],
                                   lA[:, :], lB[:, :], lC[:, :], lD[:, :], 1024,
                                   out_lo=xqlt[:, :])
                      v_xq = V.inc(v.last)
                      gp.wait_ge(V.h, v_xq)
                      STO.dma(gp.dma_start(
                          xq_sc[t * 128:(t + 1) * 128, :], xqt[:, :]))
                      stq[t] = STO.dma(gp.dma_start(
                          xql_sc[t * 128:(t + 1) * 128, :], xqlt[:, :]))
                    if "kv" in ABLATE:
                        vdt[t] = V.n
                        mmsc[t] = MM.n
                        v_sacc = V.n
                        continue
                    te.wait_ge(TP.h, tp_t)
                    if t >= 1:
                        te.wait_ge(SC.h, sc_ev_t[t - 1])
                        te.wait_ge(V.h, vdt[t - 1])
                    for dc in range(DC):
                        lhs = xn_fm[:, dc * NTOK + t * 128:
                                    dc * NTOK + (t + 1) * 128]
                        for i in range(4):
                            _mm = te.matmul(
                                kv_ps[i][:, :], lhs,
                                wqkv_sb[:, dc * 3072 + 1024 + i * 512:
                                        dc * 3072 + 1024 + (i + 1) * 512],
                                start=(dc == 0), stop=(dc == DC - 1))
                    mm_kv = MM.inc(_mm)
                    if "stats" in ABLATE:
                        vdt[t] = V.n
                        mmsc[t] = MM.n
                        v_sacc = V.n
                        sc_ev_t[t] = SC.n
                        continue
                    kvk = xnt
                    kvv = xsq
                    sc.wait_ge(MM.h, mm_kv)
                    sc.wait_ge(V.h, v_xq)
                    for i in range(2):
                        sc.activation(kvk[:, i * 512:(i + 1) * 512],
                                      kv_ps[i][:, :], AF.Copy)
                        sc.activation(kvv[:, i * 512:(i + 1) * 512],
                                      kv_ps[2 + i][:, :], AF.Copy)
                    sc_ev_t[t] = SC.inc(sc.last)
                    sc.activation(lB[:, :], kvk[:, :], AF.Square)
                    sc.activation(lD[:, :], kvv[:, :], AF.Square)
                    sc_kv = SC.inc(sc.last)
                    v.wait_ge(SC.h, sc_kv)
                    r3s = lambda ap: ap.rearrange("p (h w) -> p h w", w=64)
                    for (dst, srcb) in ((sums[:, 0:H], kvk),
                                        (sums[:, H:2 * H], kvv),
                                        (sums[:, 2 * H:3 * H], lB),
                                        (sums[:, 3 * H:4 * H], lD)):
                        v_reduce(dst, r3s(srcb[:, :]), AX.X, OP.add)
                    for (mu, r, si, qi) in ((muk, rk, 0, 2 * H),
                                            (muv, rvv, H, 3 * H)):
                        v.tensor_scalar(mu[:, :], sums[:, si:si + H],
                                        1.0 / 64, None, OP.mult)
                        v.tensor_scalar(smt[0][:, :], sums[:, qi:qi + H],
                                        1.0 / 64, None, OP.mult)
                        v.tensor_tensor(smt[1][:, :], mu[:, :], mu[:, :],
                                        OP.mult)
                        sp1()
                        v.tensor_tensor(smt[3][:, :], smt[0][:, :],
                                        smt[1][:, :], OP.subtract)
                        sp1()
                        v.tensor_scalar(smt[0][:, :], smt[3][:, :], EPS, None,
                                        OP.add)
                        if "nr" in ABLATE:
                            v.tensor_copy(r[:, :], smt[0][:, :])
                        else:
                            nr_rsqrt(smt[0][:, :], r[:, :], smt[1][:, :],
                                     smt[3][:, :])
                    if "evac" in ABLATE:
                        vdt[t] = V.inc(v.last)
                        mmsc[t] = MM.n
                        v_sacc = V.n
                        continue
                    if t >= 2:
                        v.wait_ge(MM.h, mmsc[t - 2])
                    kb, vb = KT[t % 2], VT[t % 2]
                    for (dst, mu, r, srcb) in ((kb, muk, rk, kvk),
                                               (vb, muv, rvv, kvv)):
                        v.tensor_tensor(r3h64(lA[:, :]), r3h64(srcb[:, :]),
                                        mu[:, :].unsqueeze(2)
                                        .broadcast_to((128, H, 64)),
                                        OP.subtract)
                        v.tensor_tensor(
                            dst[:, :].rearrange("p (h w) -> p h w", w=66)
                            [:, :, 0:64],
                            r3h64(lA[:, :]),
                            r[:, :].unsqueeze(2).broadcast_to((128, H, 64)),
                            OP.mult)
                    vdt[t] = V.inc(v.last)
                    if "scrs" in ABLATE:
                        mmsc[t] = MM.n
                        v_sacc = V.n
                        continue
                    te.wait_ge(V.h, vdt[t])
                    if t >= 1:
                        te.wait_ge(V.h, v_sacc)
                    for h in range(H):
                        bank, off = divmod(h, 7)
                        _mm = te.matmul(
                            st_ps[bank][0:65, off * 65:(off + 1) * 65],
                            vb[:, h * 66:h * 66 + 65],
                            kb[:, h * 66:h * 66 + 65],
                            start=True, stop=True)
                    mmsc[t] = MM.inc(_mm)
                    sc.wait_ge(MM.h, mmsc[t])
                    sc.activation(lC[0:65, 0:455], st_ps[0][0:65, 0:455],
                                  AF.Copy)
                    sc.activation(lC[0:65, 455:910], st_ps[1][0:65, 0:455],
                                  AF.Copy)
                    sc.activation(lD[0:65, 0:130], st_ps[2][0:65, 0:130],
                                  AF.Copy)
                    sc_st = SC.inc(sc.last)
                    v.wait_ge(SC.h, sc_st)
                    for bank in range(3):
                        nh = 7 if bank < 2 else 2
                        dst = sacc[t % 2][:, bank * 7 * 65:
                                          bank * 7 * 65 + nh * 65]
                        src_sb = (lC[0:65, bank * 455:bank * 455 + nh * 65]
                                  if bank < 2 else lD[0:65, 0:130])
                        if t == 0:
                            v.tensor_copy(dst[0:65, :], src_sb)
                        else:
                            v.tensor_tensor(
                                dst[0:65, :],
                                sacc[(t + 1) % 2][0:65, bank * 7 * 65:
                                                  bank * 7 * 65 + nh * 65],
                                src_sb, OP.add)
                    v_sacc = V.inc(v.last)
                if "f32:xnfm:128:1024" in taps:
                    gp.wait_ge(TP.h, TP.n)
                    STO.dma(gp.dma_start(tapd["xnfm"][:, :],
                                         xn_fm[:, 0:1024]))
                if "bar" not in ABLATE:
                    nc.all_engine_barrier()

            with ExitStack() as SS:
              if _ph("S"):
                brg_ps = ps(SS, "brg_ps")
                w2_ps = [ps(SS, f"w2p{i}") for i in range(2)]
                wstg2 = sb(SS, "wstg2", [1, 1024])
                grow = sb(SS, "grow", [1, H * DH])
                brow = sb(SS, "brow", [1, H * DH])
                gvcol = sb(SS, "gvcol", [128, H])
                bvcol = sb(SS, "bvcol", [128, H])
                nbv = sb(SS, "nbv", [128, H])
                qrow = sb(SS, "qrow", [128, H * DH])
                G_sb = sb(SS, "G_sb", [128, H * DH])
                B_sb = sb(SS, "B_sb", [128, H * DH])
                asm1 = sb(SS, "asm1", [128, DH])
                asm2 = sb(SS, "asm2", [128, DH])
                asm3 = sb(SS, "asm3", [128, DH])
                asmq = sb(SS, "asmq", [128, DH])
                asm4 = sb(SS, "asm4", [128, DH])
                colv = sb(SS, "colv", [128, 1])
                scoresT = sb(SS, "scoresT", [128, H * DH], BF16)
                SBD = sb(SS, "SBD", [128, DC * 128], BF16)

                LD.dma(gp.dma_start(grow[0:1, :],
                                    gk_d[:, :].rearrange("h w -> (h w)").unsqueeze(0)))
                LD.dma(gp.dma_start(brow[0:1, :],
                                    bk_d[:, :].rearrange("h w -> (h w)").unsqueeze(0)))
                for h in range(H):
                    LD.dma(gp.dma_start(gvcol[0:DH, h:h + 1],
                                        gv_d[h, :, None]))
                    LD.dma(gp.dma_start(bvcol[0:DH, h:h + 1],
                                        bv_d[h, :, None]))
                LD.dma(gp.dma_start(wstg2[0:1, :], bout_d[None, :]))
                sc.wait_ge(LD.h, LD.n)
                SC.inc(sc.copy(boutb[0:1, :], wstg2[0:1, :]))
                v.wait_ge(LD.h, LD.n)
                V.inc(v.tensor_scalar(nbv[0:DH, :], bvcol[0:DH, :],
                                      float(NTOK), None, OP.mult))
                te.wait_ge(LD.h, LD.n)
                for h in range(H):
                    for (src, dst) in ((grow, G_sb), (brow, B_sb)):
                        te.wait_ge(SC.h, SC.n)
                        _mm = te.matmul(brg_ps[0:DH, 0:DH], onesf[0:1, 0:DH],
                                        src[0:1, h * DH:(h + 1) * DH],
                                        start=True, stop=True)
                        MM.inc(_mm)
                        sc.wait_ge(MM.h, MM.n)
                        SC.inc(sc.activation(dst[0:DH, h * DH:(h + 1) * DH],
                                             brg_ps[0:DH, 0:DH], AF.Copy))
                sacc_fin = sacc[(NT - 1) % 2]
                def st_slot(h):
                    bank, off = divmod(h, 7)
                    return bank * 7 * 65 + off * 65
                for h in range(H):
                    o = st_slot(h)
                    V.inc(v.tensor_copy(
                        qrow[64:65, h * DH:(h + 1) * DH],
                        sacc_fin[64:65, o:o + 64]))
                sy.wait_ge(V.h, V.n)
                TP.dma(sy.dma_start(qrow[0:1, 0:H * DH],
                                    qrow[64:65, 0:H * DH]))
                te.wait_ge(TP.h, TP.n)
                v_asm = V.n
                for h in range(H):
                    te.wait_ge(SC.h, SC.n)
                    _mm = te.matmul(brg_ps[0:DH, 0:DH], onesf[0:1, 0:DH],
                                    qrow[0:1, h * DH:(h + 1) * DH],
                                    start=True, stop=True)
                    MM.inc(_mm)
                    sc.wait_ge(MM.h, MM.n)
                    sc.wait_ge(V.h, v_asm)
                    SC.inc(sc.activation(asmq[0:DH, :], brg_ps[0:DH, 0:DH],
                                         AF.Copy))
                    v.wait_ge(SC.h, SC.n)
                    o = st_slot(h)
                    sT = sacc_fin[0:DH, o:o + 64]
                    pcol = sacc_fin[0:DH, o + 64:o + 65]
                    gsl = G_sb[0:DH, h * DH:(h + 1) * DH]
                    v.tensor_scalar(colv[0:DH, :], pcol, gvcol[0:DH, h:h + 1],
                                    nbv[0:DH, h:h + 1], OP.mult, OP.add)
                    v.scalar_tensor_tensor(asm1[0:DH, :], sT,
                                           gvcol[0:DH, h:h + 1], gsl,
                                           OP.mult, OP.mult)
                    v.scalar_tensor_tensor(asm2[0:DH, :], asmq[0:DH, :],
                                           bvcol[0:DH, h:h + 1], gsl,
                                           OP.mult, OP.mult)
                    v.tensor_tensor(asm4[0:DH, :],
                                    B_sb[0:DH, h * DH:(h + 1) * DH],
                                    colv[0:DH, 0:1].broadcast_to((DH, DH)),
                                    OP.mult)
                    v.tensor_tensor(asm3[0:DH, :], asm1[0:DH, :],
                                    asm2[0:DH, :], OP.add)
                    v.tensor_tensor(asm1[0:DH, :], asm3[0:DH, :],
                                    asm4[0:DH, :], OP.add)
                    v_asm = V.inc(v.tensor_scalar(
                        scoresT[0:DH, h * DH:(h + 1) * DH], asm1[0:DH, :],
                        1.0 / NTOK, None, OP.mult))
                if "b16:scoresT:64:1024" in taps:
                    gp.wait_ge(V.h, V.n)
                    STO.dma(gp.dma_start(tapd["scoresT"][:, :],
                                         scoresT[0:DH, :]))
                gp.wait_ge(V.h, V.n)
                gp.memset(SBD[:, :], 0.0)
                GP.inc(gp.last)
                v.wait_ge(GP.h, GP.n)
                for c in range(DC):
                    V.inc(v.tensor_copy(
                        SBD[0:DH, c * 128:c * 128 + DH],
                        scoresT[0:DH, (2 * c) * DH:(2 * c + 1) * DH]))
                sy.wait_ge(V.h, V.n)
                for c in range(DC):
                    TP.dma(sy.dma_start(
                        SBD[64:128, c * 128 + 64:c * 128 + 128],
                        scoresT[0:DH, (2 * c + 1) * DH:(2 * c + 2) * DH]))
                te.wait_ge(TP.h, TP.n)
                for c in range(DC):
                    te.wait_ge(SC.h, SC.n)
                    for j in range(2):
                        _mm = te.matmul(
                            w2_ps[j][:, :], SBD[:, c * 128:(c + 1) * 128],
                            wout_sb[:, c * 1024 + j * 512:
                                    c * 1024 + (j + 1) * 512],
                            start=True, stop=True)
                    MM.inc(_mm)
                    sc.wait_ge(MM.h, MM.n)
                    for j in range(2):
                        SC.inc(sc.activation(
                            W2_sb[:, c * 1024 + j * 512:
                                  c * 1024 + (j + 1) * 512],
                            w2_ps[j][:, :], AF.Copy))
                if "b16:W2:128:8192" in taps:
                    gp.wait_ge(SC.h, SC.n)
                    STO.dma(gp.dma_start(tapd["W2"][:, :], W2_sb[:, :]))
                nc.all_engine_barrier()

            with ExitStack() as SQ:
              if _ph("Q"):
                q_ps = [ps(SQ, f"qp{i}") for i in range(2)]
                s_ps = [ps(SQ, f"sp{i}") for i in range(2)]
                qfm = sb(SQ, "qfm", [128, DC * 1024], BF16)
                stile = [sb(SQ, f"stile{i}", [128, 1024], BF16)
                         for i in range(2)]
                st_s = [0, 0]
                for sp in range(4):
                    base = sp * 1024
                    for qc in range(DC):
                        if sp or qc:
                            te.wait_ge(SC.h, SC.n)
                        for dc in range(DC):
                            lhs = wqkv_sb[:, dc * 3072 + qc * 128:
                                          dc * 3072 + (qc + 1) * 128]
                            for j in range(2):
                                _mm = te.matmul(
                                    q_ps[j][:, :], lhs,
                                    xn_fm[:, dc * NTOK + base + j * 512:
                                          dc * NTOK + base + (j + 1) * 512],
                                    start=(dc == 0), stop=(dc == DC - 1))
                        MM.inc(_mm)
                        sc.wait_ge(MM.h, MM.n)
                        for j in range(2):
                            sc.activation(
                                qfm[:, qc * 1024 + j * 512:
                                    qc * 1024 + (j + 1) * 512],
                                q_ps[j][:, :], AF.Copy)
                        SC.inc(sc.last)
                    for sub in range(8):
                        te.wait_ge(SC.h, SC.n)
                        for ec in range(DC):
                            lhs = qfm[:, ec * 1024 + sub * 128:
                                      ec * 1024 + (sub + 1) * 128]
                            for j in range(2):
                                te.matmul(s_ps[j][:, :], lhs,
                                          W2_sb[:, ec * 1024 + j * 512:
                                                ec * 1024 + (j + 1) * 512],
                                          start=(ec == 0), stop=False)
                        for j in range(2):
                            _mm = te.matmul(s_ps[j][:, :], onesb[0:1, :],
                                            boutb[0:1, j * 512:(j + 1) * 512],
                                            start=False, stop=True)
                        MM.inc(_mm)
                        sc.wait_ge(MM.h, MM.n)
                        sc.wait_ge(STO.h, st_s[sub % 2])
                        for j in range(2):
                            sc.activation(
                                stile[sub % 2][:, j * 512:(j + 1) * 512],
                                s_ps[j][:, :], AF.Copy)
                        SC.inc(sc.last)
                        gp.wait_ge(SC.h, SC.n)
                        row = sp * 1024 + sub * 128
                        st_s[sub % 2] = STO.dma(gp.dma_start(
                            s_sc[row:row + 128, :], stile[sub % 2][:, :]))
                nc.all_engine_barrier()

        STA = 512
        NSTA = NTOK // STA
        with ExitStack() as SF:
          if _ph("FA"):
            gu_ps = [ps(SF, f"gup{i}") for i in range(2)]
            WK = ("gh", "gl", "uh", "ul")
            WSRC = {"gh": wgT_sc, "gl": wgTl_sc, "uh": wuT_sc, "ul": wuTl_sc}
            wstr = [{k: sb(SF, f"ws{i}{k}", [128, DC * 512], BF16)
                     for k in WK} for i in range(2)]
            xqTh = sb(SF, "xqTh", [128, DC * STA], BF16)
            xqTl = sb(SF, "xqTl", [128, DC * STA], BF16)
            NSUB = STA // 128
            hid = sb(SF, "hid", [128, NSUB * O])
            hsq = sb(SF, "hsq", [128, 1024])
            hsgn = sb(SF, "hsgn", [128, 1024], BF16)
            hqs = [sb(SF, f"hqs{i}", [128, 1024], BF16) for i in range(2)]
            lA = sb(SF, "lA3", [128, 1024])
            lB = sb(SF, "lB3", [128, 1024])
            lC = sb(SF, "lC3", [128, 1024])
            lD = sb(SF, "lD3", [128, 1024])
            ch = {k: sb(SF, "cf_" + k, [128, 256])
                  for k in ("scv", "eacc", "ulp", "rulp", "y", "q8",
                            "dq_s", "rdq", "rdq2")}
            ch["hdq"] = sb(SF, "cf_hdq", [128, 256])
            amaxf = sb(SF, "amaxf", [128, 256])
            sg_sb = sb(SF, "sg_sb", [128, 512])
            up_sb = sb(SF, "up_sb", [128, 512])

            def load_wchunk(bi, os_):
                for k in WK:
                    n = LD.dma(sy.dma_start(
                        wstr[bi][k][:, :].rearrange("p (d w) -> p d w", d=DC),
                        WSRC[k][:, :, os_ * 512:(os_ + 1) * 512]
                        .rearrange("d p w -> p d w")))
                return n

            mm_grp = [0, 0]
            ld_w = [0, 0]
            ld_w[0] = load_wchunk(0, 0)
            sc_gu = 0
            v_hid = 0
            mm_last_st = 0
            st_slot = [0, 0]
            v_lad = 0
            for st in range(NSTA):
                r0 = st * STA
                sy.wait_ge(MM.h, mm_last_st)
                for hc in range(DC):
                    TP.dma(sy.dma_start(
                        xqTh[:, hc * STA:(hc + 1) * STA],
                        xq_sc[r0:r0 + STA, hc * 128:(hc + 1) * 128],
                        transpose=True))
                    tp_x = TP.dma(sy.dma_start(
                        xqTl[:, hc * STA:(hc + 1) * STA],
                        xql_sc[r0:r0 + STA, hc * 128:(hc + 1) * 128],
                        transpose=True))
                te.wait_ge(TP.h, tp_x)
                for os_ in range(8):
                    gi = (st * 8 + os_) % 2
                    if st * 8 + os_ + 1 < NSTA * 8:
                        sy.wait_ge(MM.h, mm_grp[1 - gi])
                        ld_w[1 - gi] = load_wchunk(1 - gi, (os_ + 1) % 8)
                    te.wait_ge(LD.h, ld_w[gi])
                    for sub in range(NSUB):
                        te.wait_ge(SC.h, sc_gu)
                        terms = (((xqTh, "gh", "uh"), (xqTh, "gl", "ul"),
                                  (xqTl, "gh", "uh")))
                        for ti, (xs, gk, uk) in enumerate(terms):
                            for dc in range(DC):
                                lhs = xs[:, dc * STA + sub * 128:
                                         dc * STA + (sub + 1) * 128]
                                st_f = (ti == 0 and dc == 0)
                                sp_f = (ti == 2 and dc == DC - 1)
                                te.matmul(gu_ps[0][:, :], lhs,
                                          wstr[gi][gk][:, dc * 512:
                                                       (dc + 1) * 512],
                                          start=st_f, stop=sp_f)
                                _mm = te.matmul(gu_ps[1][:, :], lhs,
                                                wstr[gi][uk][:, dc * 512:
                                                             (dc + 1) * 512],
                                                start=st_f, stop=sp_f)
                        mm_grp[gi] = MM.inc(_mm)
                        mm_last_st = mm_grp[gi]
                        sc.wait_ge(MM.h, mm_grp[gi])
                        sc.wait_ge(V.h, v_hid)
                        sc.activation(sg_sb[:, :], gu_ps[0][:, :], AF.Silu)
                        sc.activation(up_sb[:, :], gu_ps[1][:, :], AF.Copy)
                        sc_gu = SC.inc(sc.last)
                        v.wait_ge(SC.h, sc_gu)
                        v.tensor_tensor(
                            hid[:, sub * O + os_ * 512:
                                sub * O + (os_ + 1) * 512],
                            sg_sb[:, :], up_sb[:, :], OP.mult)
                        v_hid = V.inc(v.last)
                for sub in range(NSUB):
                    hrow = hid[:, sub * O:(sub + 1) * O]
                    v_reduce(
                        amaxf[:, :],
                        hrow.rearrange("p (k b) -> p k b", b=16),
                        AX.X, OP.max, apply_absolute_value=True)
                    V.inc(v.last)
                    scale_chain(amaxf[:, :], {k: ch[k][:, :] for k in ch},
                                gsh)
                    rr = r0 + sub * 128
                    for sl in range(4):
                        a, b = sl * 1024, (sl + 1) * 1024
                        sc.wait_ge(V.h, v_lad)
                        sc.activation(hsq[:, :], hrow[:, a:b], AF.Square)
                        sc_q = SC.inc(sc.activation(hsgn[:, :], hrow[:, a:b],
                                                    AF.Sign))
                        hb = hqs[sl % 2]
                        v.wait_ge(SC.h, sc_q)
                        v.wait_ge(STO.h, st_slot[sl % 2])
                        value_ladder(
                            hsq[:, :], hsgn[:, :],
                            ch["rdq2"][:, sl * 64:(sl + 1) * 64],
                            ch["hdq"][:, sl * 64:(sl + 1) * 64],
                            hb[:, :], lA[:, :], lB[:, :], lC[:, :],
                            lD[:, :], 1024)
                        v_lad = V.inc(v.last)
                        gp.wait_ge(V.h, v_lad)
                        st_slot[sl % 2] = STO.dma(gp.dma_start(
                            hq_sc[rr:rr + 128, a:b], hb[:, :]))
            nc.all_engine_barrier()

        with ExitStack() as SB2:
          if _ph("FB"):
            dn_ps = [ps(SB2, f"dnp{i}") for i in range(4)]
            wdT = sb(SB2, "wdT", [128, OC * 1024], BF16)
            hqT = [sb(SB2, f"hqT{i}", [128, OC * 128], BF16)
                   for i in range(2)]
            g_sb = sb(SB2, "g_sb", [128, 1024])
            sgb = sb(SB2, "sgb", [128, 1024])
            sld = [sb(SB2, f"sld{i}", [128, 1024], BF16) for i in range(2)]
            xld2 = [sb(SB2, f"xld2{i}", [128, 1024]) for i in range(2)]
            ot = [sb(SB2, f"ot{i}", [128, 1024]) for i in range(2)]

            sy.wait_ge(STO.h, STO.n)
            gp.wait_ge(STO.h, STO.n)
            for oc in range(OC):
                LD.dma(gp.dma_start(wdT[:, oc * 1024:(oc + 1) * 1024],
                                    wd_sc[oc, :, :]))
            ld_wd = LD.n
            te.wait_ge(LD.h, ld_wd)

            st_out = [0, 0]
            vd_t = [0] * (NT + 2)
            sc_ep = SC.n
            mm_t = [0] * (NT + 2)
            TP.dma(sy.dma_start(
                hqT[0][:, :].rearrange("p (m n) -> p m n", m=OC),
                hq_sc[0:128, :], transpose=True))
            tp_t = [TP.n] + [0] * (NT + 1)
            LD.dma(gp.dma_start(sld[0][:, :], s_sc[0:128, :]))
            LD.dma(gp.dma_start(xld2[0][:, :], x_d[0:128, :]))
            ld_t = [LD.n] + [0] * (NT + 1)
            for t in range(NT):
                if t + 1 < NT:
                    if t >= 1:
                        sy.wait_ge(MM.h, mm_t[t - 1])
                    rr = (t + 1) * 128
                    TP.dma(sy.dma_start(
                        hqT[(t + 1) % 2][:, :].rearrange(
                            "p (m n) -> p m n", m=OC),
                        hq_sc[rr:rr + 128, :], transpose=True))
                    tp_t[t + 1] = TP.n
                    if t >= 1:
                        gp.wait_ge(V.h, vd_t[t - 1])
                    LD.dma(gp.dma_start(sld[(t + 1) % 2][:, :],
                                        s_sc[rr:rr + 128, :]))
                    LD.dma(gp.dma_start(xld2[(t + 1) % 2][:, :],
                                        x_d[rr:rr + 128, :]))
                    ld_t[t + 1] = LD.n
                te.wait_ge(TP.h, tp_t[t])
                if t >= 2:
                    te.wait_ge(SC.h, sc_ep)
                pair = t % 2
                for oc in range(OC):
                    lhs = hqT[t % 2][:, oc * 128:(oc + 1) * 128]
                    for j in range(2):
                        _mm = te.matmul(dn_ps[pair * 2 + j][:, :], lhs,
                                        wdT[:, oc * 1024 + j * 512:
                                            oc * 1024 + (j + 1) * 512],
                                        start=(oc == 0), stop=(oc == OC - 1))
                mm_t[t] = MM.inc(_mm)
                sc.wait_ge(MM.h, mm_t[t])
                if t >= 1:
                    sc.wait_ge(V.h, vd_t[t - 1])
                for j in range(2):
                    sc.activation(g_sb[:, j * 512:(j + 1) * 512],
                                  dn_ps[pair * 2 + j][:, :], AF.Silu)
                sc_ep = SC.inc(sc.last)
                v.wait_ge(SC.h, sc_ep)
                v.wait_ge(LD.h, ld_t[t])
                if t >= 2:
                    v.wait_ge(STO.h, st_out[t % 2])
                v.tensor_tensor(sgb[:, :], g_sb[:, :],
                                sld[t % 2][:, :], OP.mult)
                v.tensor_tensor(ot[t % 2][:, :], sgb[:, :],
                                xld2[t % 2][:, :], OP.add)
                vd_t[t] = V.inc(v.last)
                gp.wait_ge(V.h, vd_t[t])
                st_out[t % 2] = STO.dma(gp.dma_start(
                    out_d[t * 128:(t + 1) * 128, :], ot[t % 2][:, :]))
            gp.wait_ge(STO.h, STO.n)
        gp.wait_ge(STO.h, STO.n)
        gp.wait_ge(TP.h, TP.n)
        gp.wait_ge(V.h, V.n)
        gp.wait_ge(SC.h, SC.n)
        gp.wait_ge(MM.h, MM.n)
    return nc


_CACHE = {}


def kernel(**inputs):
    def f32(name):
        return np.ascontiguousarray(np.asarray(inputs[name], dtype=np.float32))

    x = f32("L0_attn_x")
    base = {
        "wqkv": f32("L0_attn_w_qkv"),
        "gk": f32("L0_attn_gamma_k"), "bk": f32("L0_attn_beta_k"),
        "gv": f32("L0_attn_gamma_v"), "bv": f32("L0_attn_beta_v"),
        "wout": f32("L0_attn_w_out"), "bout": f32("L0_attn_b_out"),
        "gate": f32("L0_ffn_gate_w"), "up": f32("L0_ffn_up_w"),
        "down": f32("L0_ffn_down_w"),
        "sin": f32("L0_ffn_s_in"), "sind": f32("L0_ffn_s_in_down"),
    }
    if "nc" not in _CACHE:
        _CACHE["nc"] = build()
    nc = _CACHE["nc"]
    in_maps = [dict(base, x=np.ascontiguousarray(x[c])) for c in range(8)]
    res = run_bass_kernel_spmd(nc, in_maps, core_ids=list(range(8)))
    return np.stack([res.results[c]["out"] for c in range(8)], axis=0)
